# version 19
# speedup vs baseline: 1.0920x; 1.0174x over previous
import bisect
import numpy as np
from contextlib import ExitStack

import concourse.bass as bass
import concourse.mybir as mybir
from concourse.bass_utils import run_bass_kernel_spmd

F32 = mybir.dt.float32
BF16 = mybir.dt.bfloat16
I32 = mybir.dt.int32
I16 = mybir.dt.int16
U16 = mybir.dt.uint16
AF = mybir.ActivationFunctionType
ALU = mybir.AluOpType
AX = mybir.AxisListType

TOK = 2048
D = 1024
DC = 8
NH = 16
DH = 64
II = 3072
IC = 24
NTT = 4
NTS = 16
QMAX = 32767.0
EPS_LN = 1e-5
VW = NH * DH + NH

P_HALF, P_EPSLN, P_INVSA, P_SG, P_INVG, P_INVU, P_INVG2, P_INVO, P_HALFG = range(9)
P_BOUND0 = 16
NPOOL = 32

(S_CAL, S_HALF, S_GM, S_GM2, S_SA, S_TMP, S_SGV, S_SUV, S_C2, S_C3, S_T2,
 S_T3, S_T4, S_T5) = range(14)

ENGS = ("sync", "tensor", "vector", "scalar", "gpsimd")

RELAXED = True
RELAX_ENGS = ("tensor",)


class Op:
    __slots__ = ("eng", "fn", "deps", "kind", "stream", "idx", "published", "semval")

    def __init__(self, eng, fn, deps, kind, stream):
        self.eng, self.fn, self.deps = eng, fn, deps
        self.kind, self.stream = kind, stream
        self.published = False
        self.semval = None


class Buf:
    __slots__ = ("ws", "rs", "name")

    def __init__(self, name=""):
        self.ws = []
        self.rs = []
        self.name = name


class KB:

    def __init__(self, nc, es):
        self.nc, self.es = nc, es
        self.ops = []
        self.phases = []
        self.cur_start = 0
        self.eng_sem = {e: es.enter_context(nc.semaphore(f"es_{e}")) for e in ENGS}
        self.dma_sems = {}

    def _stream(self, name):
        if name not in self.dma_sems:
            self.dma_sems[name] = self.es.enter_context(self.nc.semaphore(f"dq_{name}"))
        return name

    def rec(self, eng, fn, reads=(), writes=(), ptr_reads=(), kind="compute",
            stream=None):
        deps = []
        for b in reads:
            deps.extend((w, False) for w in b.ws)
        for b in ptr_reads:
            deps.extend((w, True) for w in b.ws)
        for b in writes:
            deps.extend((r, False) for r in b.rs)
            deps.extend((w, False) for w in b.ws)
        op = Op(eng, fn, deps, kind, stream)
        op.idx = len(self.ops)
        self.ops.append(op)
        for b in reads:
            b.rs.append(op)
        for b in ptr_reads:
            b.rs.append(op)
        for b in writes:
            b.ws.append(op)
            b.rs = []
        return op

    def dma(self, eng, stream, fn, reads=(), writes=()):
        return self.rec(eng, fn, reads=reads, writes=writes, kind="dma",
                        stream=self._stream(stream))

    def cc(self, fn, reads=(), writes=()):
        return self.rec("gpsimd", fn, reads=reads, writes=writes, kind="cc",
                        stream=self._stream("ccs"))

    def end_phase(self):
        if self.cur_start != len(self.ops):
            dmas = [o for o in self.ops[self.cur_start:] if o.kind != "compute"]
            if dmas:
                op = Op("gpsimd", lambda g: g.engine_nop(),
                        [(d, False) for d in dmas], "compute", None)
                op.idx = len(self.ops)
                self.ops.append(op)
            self.phases.append((self.cur_start, len(self.ops)))
            self.cur_start = len(self.ops)

    def finalize(self):
        self.end_phase()
        for op in self.ops:
            for dep, is_ptr in op.deps:
                if (RELAXED and dep.kind == "compute" and dep.eng == op.eng
                        and not is_ptr and op.eng in RELAX_ENGS):
                    continue
                dep.published = True
        cnt = {e: 0 for e in ENGS}
        scnt = {}
        for op in self.ops:
            if op.kind == "compute":
                if op.published:
                    cnt[op.eng] += 1
                    op.semval = (self.eng_sem[op.eng], cnt[op.eng])
            else:
                step = 16 if op.kind == "dma" else 1
                k = scnt.get(op.stream, 0) + step
                scnt[op.stream] = k
                op.semval = (self.dma_sems[op.stream], k)
                op.published = True
        stream_idx = {}
        for op in self.ops:
            if op.kind != "compute":
                stream_idx.setdefault(op.stream, []).append(op.idx)
        for (a, b) in self.phases:
            ops = self.ops[a:b]
            byeng = {e: [o for o in ops if o.eng == e] for e in ENGS}
            with self.nc.Block() as blk:
                for e in ENGS:
                    if not byeng[e]:
                        continue

                    def emit_all(eng_h, elist=byeng[e], ename=e,
                                 marks={}):
                        for op in elist:
                            want = {}
                            for dep, is_ptr in op.deps:
                                if dep.semval is None:
                                    continue
                                if (RELAXED and dep.kind == "compute"
                                        and dep.eng == ename and not is_ptr
                                        and ename in RELAX_ENGS):
                                    continue
                                semh, val = dep.semval
                                if dep.kind != "compute":
                                    step = 16 if dep.kind == "dma" else 1
                                    n = bisect.bisect_left(
                                        stream_idx[dep.stream], op.idx)
                                    val = step * n
                                k = id(semh)
                                if val > want.get(k, (None, 0))[1]:
                                    want[k] = (semh, val)
                            for k, (semh, val) in want.items():
                                key = (ename, k)
                                if marks.get(key, 0) >= val:
                                    continue
                                eng_h.wait_ge(semh, val)
                                marks[key] = val
                            try:
                                ins = op.fn(eng_h)
                            except Exception:
                                print(f"EMIT FAIL op idx={op.idx} eng={op.eng} "
                                      f"kind={op.kind}")
                                raise
                            if op.semval is not None:
                                step = 16 if op.kind == "dma" else 1
                                ins.then_inc(op.semval[0], step)

                    getattr(blk, e)(emit_all)


class Plc:

    def __init__(self, nc, es):
        self.nc, self.es = nc, es
        self.virt = []
        self.size = 0

    def at(self, name, shape, dtype, addr):
        h = self.nc.virtual_sbuf_tensor(name, list(shape), dtype)
        per_part = int(np.prod(shape[1:])) * mybir.dt.size(dtype)
        self.size = max(self.size, addr + per_part)
        self.virt.append((h, addr))
        return h

    def finalize(self):
        backing = self.es.enter_context(self.nc.sbuf_tensor(
            "ARENA", [128, self.size], mybir.dt.uint8))
        base = self.nc.lookup_mloc(backing).addr
        for h, addr in self.virt:
            self.nc._tensor(h.name, list(h.shape), h.dtype, type="SB")
            mloc = self.nc.lookup_mloc(h)
            mloc.allocated = not self.nc.target_bir_lowering
            mloc.addr = base + addr


def build(tap=None):
    nc = bass.Bass("TRN2", target_bir_lowering=False,
                   detect_race_conditions=not RELAXED)
    es = ExitStack()
    kb = KB(nc, es)
    A = Plc(nc, es)

    x_fm = nc.dram_tensor("x_fm", [D, TOK], F32, kind="ExternalInput")
    wqkv = [nc.dram_tensor(n, [D, D], F32, kind="ExternalInput")
            for n in ("wq", "wk", "wv")]
    wffn_n = {
        "up": nc.dram_tensor("wup_n", [II, D], F32, kind="ExternalInput"),
        "gate": nc.dram_tensor("wgate_n", [II, D], F32, kind="ExternalInput"),
        "down": nc.dram_tensor("wdown_n", [D, II], F32, kind="ExternalInput"),
    }
    out_fm = nc.dram_tensor("out_fm", [D, TOK], F32, kind="ExternalOutput")
    dbg = None
    if tap is not None:
        dbg = nc.dram_tensor("dbg", [128, NTS * VW], F32, kind="ExternalOutput")

    qk_sp = nc.dram_tensor("qk_sp", [2, DC, 2, 128, TOK], BF16)
    wt_sp = {
        "up": nc.dram_tensor("wup_t", [IC, 2, DC, 128, 128], BF16),
        "gate": nc.dram_tensor("wgate_t", [IC, 2, DC, 128, 128], BF16),
        "down": nc.dram_tensor("wdown_t", [DC, 2, IC, 128, 128], BF16),
    }
    actkind = "Internal" if tap is None else "ExternalOutput"
    act_sp = {
        "up": nc.dram_tensor("up_i", [IC, 128, TOK], F32, kind=actkind),
        "gate": nc.dram_tensor("gate_i", [IC, 128, TOK], F32, kind=actkind),
        "gate2": nc.dram_tensor("gate2_i", [IC, 128, TOK], F32, kind=actkind),
        "o": nc.dram_tensor("o_i", [IC, 128, TOK], F32, kind=actkind),
    }
    x2_sp = nc.dram_tensor("x2_sp", [DC, 128, TOK], F32)
    cc_in = nc.dram_tensor("cc_in", [1, 8], F32)
    cc_out = nc.dram_tensor("cc_out", [1, 8], F32)

    o = [0]

    def nxt(sz):
        a = (o[0] + 63) & ~63
        o[0] = a + sz
        return a

    pool = A.at("pool", (128, NPOOL), F32, nxt(NPOOL * 4))
    ident = A.at("ident", (128, 128), F32, nxt(512))
    identb = A.at("identb", (128, 128), BF16, nxt(256))
    ones_col = A.at("ones_col", (128, 1), F32, nxt(4))
    ones_row = A.at("ones_row", (1, 128), F32, nxt(512))
    s11 = A.at("s11", (1, 16), F32, nxt(64))
    i11 = A.at("i11", (1, 4), I32, nxt(16))
    ccst = A.at("ccst", (1, 8), F32, nxt(32))
    stat = A.at("stat", (128, 7 * NTS), F32, nxt(7 * NTS * 4))
    stat_cat = A.at("stat_cat", (128, 2 * NTS), F32, nxt(2 * NTS * 4))
    rowstage = A.at("rowstage", (32, 128), F32, nxt(512))
    rr4 = A.at("rr4", (1, 512), F32, nxt(2048))
    mr4 = A.at("mr4", (1, 512), F32, nxt(2048))
    gmax_acc = A.at("gmax_acc", (128, 4), F32, nxt(16))
    red1 = A.at("red1", (1, 512), F32, nxt(2048))
    sqt = [A.at(f"sqt{i}", (128, 128), F32, nxt(512)) for i in range(2)]
    sq5 = [A.at(f"sq5{i}", (128, 512), F32, nxt(2048)) for i in range(2)]
    red2 = A.at("red2", (1, 512), F32, nxt(2048))
    ust = [A.at(f"ust{i}", (128, 512), F32, nxt(2048)) for i in range(4)]
    qkmax = A.at("qkmax", (128, 16), F32, nxt(64))
    sdq = [A.at(f"sdq{i}", (128, 96), F32, nxt(384)) for i in range(2)]
    rdq = [A.at(f"rdq{i}", (128, 96), F32, nxt(384)) for i in range(2)]
    PERS_END = (o[0] + 63) & ~63

    AR1 = PERS_END
    AR2 = AR1 + 66560
    AR3 = AR2 + 65536

    x_sb = A.at("x_sb", (128, DC * TOK), F32, AR1)
    vsb = A.at("vsb", (128, NTS * VW), F32, AR1)
    x2_sb = A.at("x2_sb", (128, DC * TOK), F32, AR1)
    wnat = [A.at(f"wnat{i}", (128, II), F32, AR1 + i * 12288) for i in range(2)]
    vt = A.at("vt", (128, II), F32, AR1 + 24576)
    qt_ = A.at("qt_", (128, II), I32, AR1 + 36864)
    wdqt = A.at("wdqt", (128, II), F32, AR1 + 49152)
    gsb = [A.at(f"gsb{i}", (128, TOK), F32, AR1 + i * 8192) for i in range(2)]
    usb = [A.at(f"usb{i}", (128, TOK), F32, AR1 + 16384 + i * 8192) for i in range(2)]
    sgf = A.at("sgf", (128, TOK), F32, AR1 + 32768)
    ngi = A.at("ngi", (128, TOK), I32, AR1 + 40960)
    sigf = A.at("sigf", (128, TOK), F32, AR1 + 49152)
    mi = A.at("mi", (128, TOK), I32, AR1 + 57344)
    g2f = [A.at(f"g2f{i}", (128, TOK), F32, AR1 + 16384 + i * 8192) for i in range(2)]
    nui = A.at("nui", (128, TOK), I32, AR1 + 57344)
    wdsbA = A.at("wdsbA", (128, 4 * 2 * IC * 128), BF16, AR1)
    ysb = A.at("ysb", (128, DC * 512), F32, AR1 + 49152)

    xh_sb = A.at("xh_sb", (128, DC * TOK), BF16, AR2)
    xl_sb = A.at("xl_sb", (128, DC * TOK), BF16, AR2 + 32768)
    attn_sb = A.at("attn_sb", (128, DC * TOK), F32, AR2)
    nxh_sb = A.at("nxh_sb", (128, DC * TOK), BF16, AR2)
    nxl_sb = A.at("nxl_sb", (128, DC * TOK), BF16, AR2 + 32768)
    of = [A.at(f"of{i}", (128, TOK), F32, AR2 + i * 8192) for i in range(2)]
    wdsbB = A.at("wdsbB", (128, 4 * 2 * IC * 128), BF16, AR2)
    ost = [A.at(f"ost{i}", (128, 512), F32, AR2 + 49152 + i * 2048) for i in range(2)]
    sgo = [A.at(f"sgo{i}", (128, 512), F32, AR2 + 53248 + i * 2048) for i in range(2)]
    noi = [A.at(f"noi{i}", (128, 512), I32, AR2 + 57344 + i * 2048) for i in range(2)]

    r_row = A.at("r_row", (1, TOK), F32, AR3)
    mr_row = A.at("mr_row", (1, TOK), F32, AR3 + 8192)
    rw = [A.at(f"rw{i}", (1, TOK), F32, AR3 + 16384 + i * 8192) for i in range(2)]
    nxi = [A.at(f"nxi{i}", (128, 512), I32, AR3 + 32768 + i * 2048) for i in range(2)]
    wsb = [A.at(f"wsb{i}", (128, 1024), F32, AR3 + i * 4096) for i in range(2)]
    wh_sb = A.at("wh_sb", (128, DC * 1024), BF16, AR3 + 8192)
    wl_sb = A.at("wl_sb", (128, DC * 1024), BF16, AR3 + 24576)
    qst_h = [A.at("qst_h0", (128, TOK), BF16, AR3 + 40960)]
    qst_l = [A.at("qst_l0", (128, TOK), BF16, AR3 + 45056)]
    qbuf = {}
    for i, nm in enumerate(("qh", "ql", "kh", "kl")):
        for par in range(2):
            qbuf[(nm, par)] = A.at(f"qb_{nm}{par}", (64, TOK), BF16,
                                   AR3 + (par * 4 + i) * 4096)
    pexp = [A.at(f"pexp{i}", (128, 512), F32, AR3 + 32768 + i * 2048) for i in range(4)]
    zst = [A.at(f"zst{i}", (64, 512), F32, AR3 + 40960 + i * 2048) for i in range(2)]
    zot = [A.at(f"zot{i}", (64, 512), F32, AR3 + 45056 + i * 2048) for i in range(2)]
    sgt = A.at("sgt", (128, II), F32, AR3)
    sgt2 = A.at("sgt2", (128, II), F32, AR3 + 12288)
    wts = [A.at(f"wts{i}", (128, 2 * IC * 128), BF16, AR3 + 24576 + i * 12288)
           for i in range(2)]
    wstr = [A.at(f"wstr{i}", (128, 2 * DC * 128), BF16, AR3 + i * 4096) for i in range(2)]
    fstg = [A.at(f"fstg{i}", (128, TOK), F32, AR3 + 8192 + i * 8192) for i in range(2)]
    noh = A.at("noh", (128, IC * 512), BF16, AR3)
    nol = A.at("nol", (128, IC * 512), BF16, AR3 + 24576)

    pb = [es.enter_context(nc.psum_tensor(f"pb{i}", [128, 512], F32))
          for i in range(5)]
    pbb = [es.enter_context(nc.psum_tensor(f"pbb{i}", [128, 1024], BF16))
           for i in range(2)]

    ST_SUM, ST_SQ, ST_MEAN, ST_VAR, ST_R, ST_MR, ST_T1 = range(7)

    def stslc(which, n=NTS, base=0):
        return stat[:, which * NTS + base: which * NTS + base + n]

    B = {}

    def gb(name):
        if name not in B:
            B[name] = Buf(name)
        return B[name]

    kb.rec("gpsimd", lambda g: g.memset(ident[:, :], 0.0), writes=[gb("ident")])
    kb.rec("gpsimd", lambda g: g.affine_select(
        out=ident[:, :], in_=ident[:, :], compare_op=ALU.not_equal, fill=1.0,
        base=0, pattern=[[-1, 128]], channel_multiplier=1),
        reads=[gb("ident")], writes=[gb("ident")])
    kb.rec("vector", lambda v: v.memset(pool[:, :], 0.0), writes=[gb("pool")])
    kb.rec("vector", lambda v: v.memset(ones_col[:, :], 1.0), writes=[gb("ones_col")])
    kb.rec("vector", lambda v: v.memset(ones_row[:, :], 1.0), writes=[gb("ones_row")])
    kb.rec("vector", lambda v: v.memset(s11[:, :], 0.0), writes=[gb("s11")])
    kb.rec("vector", lambda v: v.memset(s11[0:1, S_CAL:S_CAL + 1], 2.7),
           reads=[gb("s11")], writes=[gb("s11")])
    kb.rec("vector", lambda v: v.tensor_copy(i11[0:1, 0:1], s11[0:1, S_CAL:S_CAL + 1]),
           reads=[gb("s11")], writes=[gb("i11")])
    kb.rec("vector", lambda v: v.tensor_scalar(
        s11[0:1, S_HALF:S_HALF + 1], i11[0:1, 0:1], -0.5, 1.5, ALU.mult, ALU.add),
        reads=[gb("i11"), gb("s11")], writes=[gb("s11")])
    kb.rec("vector", lambda v: v.tensor_copy(identb[:, :], ident[:, :]),
           reads=[gb("ident")], writes=[gb("identb")])
    kb.rec("gpsimd", lambda g: g.tensor_copy(i11[0:1, 1:2], s11[0:1, S_CAL:S_CAL + 1]),
           reads=[gb("s11")], writes=[gb("i11")])
    kb.rec("vector", lambda v: v.tensor_scalar(
        s11[0:1, S_T2:S_T2 + 1], i11[0:1, 1:2], -0.5, 1.5, ALU.mult, ALU.add),
        reads=[gb("i11"), gb("s11")], writes=[gb("s11")])

    def bcast(slot, col):
        kb.rec("tensor", lambda t: t.matmul(
            pb[2][:, 0:1], ones_row[0:1, :], s11[0:1, slot:slot + 1],
            start=True, stop=True),
            reads=[gb("s11"), gb("ones_row")], writes=[gb("PB2")])
        kb.rec("vector", lambda v: v.tensor_copy(pool[:, col:col + 1], pb[2][:, 0:1]),
               reads=[gb("PB2")], writes=[gb("pool")])

    bcast(S_HALF, P_HALF)
    bcast(S_T2, P_HALFG)
    kb.rec("vector", lambda v: v.memset(pool[:, P_EPSLN:P_EPSLN + 1], EPS_LN),
           reads=[gb("pool")], writes=[gb("pool")])
    kb.end_phase()

    def emit_ln(name, src, srcB, apply_fn):
        statB = gb("stat")
        sumB, sqpB = gb("PB0"), gb("PB1")
        rowAB, rowBB = gb("r_row"), gb("mr_row")
        rw0B, rw1B = gb("rw0"), gb("rw1")
        for tt in range(NTT):
            for dc in range(DC):
                sqb = gb(f"sq5{dc % 2}")
                o = dc * TOK + tt * 512
                kb.rec("scalar", lambda s, o=o, dc=dc: s.activation(
                    out=sq5[dc % 2][:, :], in_=src[:, o:o + 512],
                    func=AF.Square), reads=[srcB], writes=[sqb])
                kb.rec("tensor", lambda t, o=o, dc=dc: t.matmul(
                    pb[0][0:1, :], ones_col[:, :], src[:, o:o + 512],
                    start=(dc == 0), stop=(dc == DC - 1)),
                    reads=[srcB, gb("ones_col")], writes=[sumB])
                kb.rec("tensor", lambda t, dc=dc: t.matmul(
                    pb[1][0:1, :], ones_col[:, :], sq5[dc % 2][:, :],
                    start=(dc == 0), stop=(dc == DC - 1)),
                    reads=[sqb, gb("ones_col")], writes=[sqpB])
            kb.rec("vector", lambda v, tt=tt: v.tensor_copy(
                r_row[0:1, tt * 512:(tt + 1) * 512], pb[0][0:1, :]),
                reads=[sumB], writes=[rowAB])
            kb.rec("vector", lambda v, tt=tt: v.tensor_copy(
                mr_row[0:1, tt * 512:(tt + 1) * 512], pb[1][0:1, :]),
                reads=[sqpB], writes=[rowBB])
        ep = pool[0:1, P_EPSLN:P_EPSLN + 1]
        for f, rd, wr, pr in [
            (lambda v: v.tensor_scalar(rw[0][0:1, :], r_row[0:1, :], 1.0 / D, None, ALU.mult),
             [rowAB], [rw0B], []),
            (lambda v: v.tensor_scalar(mr_row[0:1, :], mr_row[0:1, :], 1.0 / D, None, ALU.mult),
             [rowBB], [rowBB], []),
        ]:
            kb.rec("vector", f, reads=rd, writes=wr, ptr_reads=pr)
        kb.rec("scalar", lambda s: s.activation(
            out=r_row[0:1, :], in_=rw[0][0:1, :], func=AF.Square),
            reads=[rw0B], writes=[rowAB])
        kb.rec("vector", lambda v: v.tensor_tensor(
            mr_row[0:1, :], mr_row[0:1, :], r_row[0:1, :], ALU.subtract),
            reads=[rowBB, rowAB], writes=[rowBB])
        kb.rec("scalar", lambda s: s.activation(
            out=r_row[0:1, :], in_=mr_row[0:1, :], func=AF.Sqrt, bias=ep),
            reads=[rowBB], ptr_reads=[gb("pool")], writes=[rowAB])
        kb.rec("vector", lambda v: v.reciprocal(rw[1][0:1, :], r_row[0:1, :]),
               reads=[rowAB], writes=[rw1B])
        kb.rec("vector", lambda v: v.tensor_scalar(
            r_row[0:1, :], mr_row[0:1, :], 1.0, ep, ALU.mult, ALU.add),
            reads=[rowBB], ptr_reads=[gb("pool")], writes=[rowAB])
        kb.rec("vector", lambda v: v.tensor_tensor(
            mr_row[0:1, :], rw[1][0:1, :], rw[1][0:1, :], ALU.mult),
            reads=[rw1B], writes=[rowBB])
        kb.rec("vector", lambda v: v.tensor_tensor(
            mr_row[0:1, :], mr_row[0:1, :], r_row[0:1, :], ALU.mult),
            reads=[rowBB, rowAB], writes=[rowBB])
        kb.rec("vector", lambda v: v.tensor_scalar(
            mr_row[0:1, :], mr_row[0:1, :], -0.5, 1.5, ALU.mult, ALU.add),
            reads=[rowBB], writes=[rowBB])
        kb.rec("vector", lambda v: v.tensor_tensor(
            rw[1][0:1, :], rw[1][0:1, :], mr_row[0:1, :], ALU.mult),
            reads=[rw1B, rowBB], writes=[rw1B])
        kb.rec("vector", lambda v: v.tensor_copy(r_row[0:1, :], rw[1][0:1, :]),
               reads=[rw1B], writes=[rowAB])
        kb.rec("vector", lambda v: v.tensor_tensor(
            mr_row[0:1, :], rw[0][0:1, :], rw[1][0:1, :], ALU.mult),
            reads=[rw0B, rw1B], writes=[rowBB])
        rB, mrB = gb("PB3"), gb("PB4")
        for tt in range(NTT):
            kb.rec("tensor", lambda t, tt=tt: t.matmul(
                pb[3][:, :], ones_row[0:1, :], r_row[0:1, tt * 512:(tt + 1) * 512],
                start=True, stop=True),
                reads=[rowAB, gb("ones_row")], writes=[rB])
            kb.rec("tensor", lambda t, tt=tt: t.matmul(
                pb[4][:, :], ones_row[0:1, :], mr_row[0:1, tt * 512:(tt + 1) * 512],
                start=True, stop=True),
                reads=[rowBB, gb("ones_row")], writes=[mrB])
            for dc in range(DC):
                apply_fn(tt, dc, rB, mrB)

    def cpmax(src_ap, ncols, slots, bufs):
        for j, slot in enumerate(slots):
            kb.rec("tensor", lambda tt_, j=j: tt_.transpose(
                pb[2][0:1, 0:128], gmax_acc[:, j:j + 1], ident[:, :]),
                reads=list(bufs) + [gb("ident")], writes=[gb("PB2")])
            kb.rec("vector", lambda v, slot=slot: v.tensor_reduce(
                s11[0:1, slot:slot + 1], pb[2][0:1, 0:128],
                AX.X, ALU.max, apply_absolute_value=True),
                reads=[gb("PB2")], writes=[gb("s11")])

    def barrier(nvals):
        kb.dma("gpsimd", "ccio", lambda g: g.dma_start(out=cc_in[:, :], in_=ccst[:, :]),
               reads=[gb("ccst")], writes=[gb("cc_in")])
        kb.cc(lambda g: g.collective_compute(
            "AllReduce", ALU.max, replica_groups=[list(range(8))],
            ins=[cc_in.ap()], outs=[cc_out.ap()]),
            reads=[gb("cc_in")], writes=[gb("cc_out")])
        kb.dma("gpsimd", "ccio", lambda g: g.dma_start(out=ccst[:, :], in_=cc_out[:, :]),
               reads=[gb("cc_out")], writes=[gb("ccst")])

    for dc in range(DC):
        kb.dma("sync", "xin", lambda s, dc=dc: s.dma_start(
            out=x_sb[:, dc * TOK:(dc + 1) * TOK],
            in_=x_fm[dc * 128:(dc + 1) * 128, :]), writes=[gb("x_sb")])

    def p1_apply(tt, dc, rB, mrB):
        o = dc * TOK + tt * 512
        u = ust[dc % 2]
        uB = gb(f"ust{dc % 2}")
        kb.rec("vector", lambda v: v.tensor_tensor(
            u[:, :], x_sb[:, o:o + 512], pb[3][:, :], ALU.mult),
            reads=[gb("x_sb"), rB], writes=[uB])
        kb.rec("vector", lambda v: v.tensor_tensor(
            u[:, :], u[:, :], pb[4][:, :], ALU.subtract),
            reads=[uB, mrB], writes=[uB])
        kb.rec("vector", lambda v: v.tensor_copy(xh_sb[:, o:o + 512], u[:, :]),
               reads=[uB], writes=[gb("xh")])
        kb.rec("vector", lambda v: v.tensor_tensor(
            xl_sb[:, o:o + 512], u[:, :], xh_sb[:, o:o + 512], ALU.subtract),
            reads=[uB, gb("xh")], writes=[gb("xl")])

    emit_ln("p1", x_sb, gb("x_sb"), p1_apply)
    kb.end_phase()

    if tap == "ln1":
        for dc in range(DC):
            kb.rec("vector", lambda v, dc=dc: v.tensor_tensor(
                x_sb[:, dc * TOK:(dc + 1) * TOK], xh_sb[:, dc * TOK:(dc + 1) * TOK],
                xl_sb[:, dc * TOK:(dc + 1) * TOK], ALU.add),
                reads=[gb("xh"), gb("xl")], writes=[gb("x_sb")])
        _tapd(kb, gb, dbg, [(x_sb, gb("x_sb"), DC * TOK)])
        A.finalize()
        kb.finalize()
        nc._keep = (es,)
        return nc

    kb.rec("vector", lambda v: v.memset(vsb[:, :], 1.0), writes=[gb("vsb")])

    for p in range(3):
        w_hbm = wqkv[p]
        for dc in range(DC):
            wb = gb(f"wsb{dc % 2}")
            kb.dma("sync", f"wld{dc % 2}", lambda s, dc=dc, w_hbm=w_hbm: s.dma_start(
                out=wsb[dc % 2][:, :], in_=w_hbm[dc * 128:(dc + 1) * 128, :]),
                writes=[wb])
            kb.rec("scalar", lambda s, dc=dc: s.activation(
                out=wh_sb[:, dc * 1024:(dc + 1) * 1024], in_=wsb[dc % 2][:, :],
                func=AF.Copy), reads=[wb], writes=[gb("wh")])
            kb.rec("vector", lambda v, dc=dc: v.tensor_tensor(
                wl_sb[:, dc * 1024:(dc + 1) * 1024], wsb[dc % 2][:, :],
                wh_sb[:, dc * 1024:(dc + 1) * 1024], ALU.subtract),
                reads=[wb, gb("wh")], writes=[gb("wl")])
        if p < 2:
            for ec in range(DC):
                sh, sl = qst_h[0], qst_l[0]
                shB, slB = gb("qsth"), gb("qstl")
                pk = gb(f"PB{ec % 2}")
                for tt in range(NTT):
                    mms = []
                    for wbuf, wB, xbuf, xB in (
                            (wh_sb, gb("wh"), xh_sb, gb("xh")),
                            (wh_sb, gb("wh"), xl_sb, gb("xl")),
                            (wl_sb, gb("wl"), xh_sb, gb("xh"))):
                        for dc in range(DC):
                            mms.append((wbuf, wB, xbuf, xB, dc))
                    for i, (wbuf, wB, xbuf, xB, dc) in enumerate(mms):
                        kb.rec("tensor", lambda t, wbuf=wbuf, xbuf=xbuf, dc=dc,
                               i=i, ec=ec, tt=tt, n=len(mms): t.matmul(
                                   pb[ec % 2][:, :],
                                   wbuf[:, dc * 1024 + ec * 128: dc * 1024 + ec * 128 + 128],
                                   xbuf[:, dc * TOK + tt * 512: dc * TOK + tt * 512 + 512],
                                   start=(i == 0), stop=(i == n - 1)),
                            reads=[wB, xB], writes=[pk])
                    kb.rec("scalar", lambda s, tt=tt, ec=ec: s.activation(
                        out=sh[:, tt * 512:(tt + 1) * 512], in_=pb[ec % 2][:, :],
                        func=AF.Copy), reads=[pk], writes=[shB])
                    kb.rec("vector", lambda v, tt=tt, ec=ec: v.tensor_tensor(
                        sl[:, tt * 512:(tt + 1) * 512], pb[ec % 2][:, :],
                        sh[:, tt * 512:(tt + 1) * 512], ALU.subtract),
                        reads=[pk, shB], writes=[slB])
                kb.rec("vector", lambda v, p=p, ec=ec: v.tensor_reduce(
                    qkmax[:, p * DC + ec: p * DC + ec + 1], sh[:, :], AX.X,
                    ALU.max, apply_absolute_value=True),
                    reads=[shB], writes=[gb("qkmax")])
                offh = ((p * DC + ec) * 2 + 0) * 128 * TOK
                offl = ((p * DC + ec) * 2 + 1) * 128 * TOK
                kb.dma("gpsimd", "qksp", lambda g, offh=offh, sh=sh: g.dma_start(
                    out=bass.AP(qk_sp, offh, [[TOK, 128], [1, TOK]]), in_=sh[:, :]),
                    reads=[shB], writes=[gb(f"qksp{p}_{ec}")])
                kb.dma("gpsimd", "qksp", lambda g, offl=offl, sl=sl: g.dma_start(
                    out=bass.AP(qk_sp, offl, [[TOK, 128], [1, TOK]]), in_=sl[:, :]),
                    reads=[slB], writes=[gb(f"qksp{p}_{ec}")])
        else:
            for tc in range(NTS):
                for eh in range(2):
                    pk = gb(f"PB{(tc * 2 + eh) % 2}")
                    mms = []
                    for xbuf, xB, wbuf, wB in (
                            (xh_sb, gb("xh"), wh_sb, gb("wh")),
                            (xh_sb, gb("xh"), wl_sb, gb("wl")),
                            (xl_sb, gb("xl"), wh_sb, gb("wh"))):
                        for dc in range(DC):
                            mms.append((xbuf, xB, wbuf, wB, dc))
                    for i, (xbuf, xB, wbuf, wB, dc) in enumerate(mms):
                        kb.rec("tensor", lambda t, xbuf=xbuf, wbuf=wbuf, dc=dc,
                               i=i, tc=tc, eh=eh, n=len(mms): t.matmul(
                                   pb[(tc * 2 + eh) % 2][:, :],
                                   xbuf[:, dc * TOK + tc * 128: dc * TOK + tc * 128 + 128],
                                   wbuf[:, dc * 1024 + eh * 512: dc * 1024 + eh * 512 + 512],
                                   start=(i == 0), stop=(i == n - 1)),
                            reads=[xB, wB], writes=[pk])
                    kb.rec("scalar", lambda s, tc=tc, eh=eh: s.activation(
                        out=bass.AP(vsb, tc * VW + eh * 8 * 65,
                                    [[NTS * VW, 128], [65, 8], [1, 64]]),
                        in_=pb[(tc * 2 + eh) % 2][:, :], func=AF.Copy),
                        reads=[pk, gb("vsb")], writes=[gb("vsb")])

    cpmax_src = qkmax[:, 0:16]
    kb.rec("tensor", lambda t: t.transpose(
        pb[2][0:16, 0:128], cpmax_src, ident[:, :]),
        reads=[gb("qkmax"), gb("ident")], writes=[gb("PB2")])
    kb.rec("vector", lambda v: v.tensor_copy(
        rowstage[0:16, :], pb[2][0:16, 0:128]),
        reads=[gb("PB2")], writes=[gb("rowstage")])
    kb.dma("gpsimd", "rows", lambda g: g.dma_start(
        out=bass.AP(r_row, 0, [[TOK, 1], [128, 16], [1, 128]]),
        in_=rowstage[0:16, :]),
        reads=[gb("rowstage")], writes=[gb("r_row")])
    for h in range(NH):
        oq = (h // 2) * 128 + 64 * (h % 2)
        ok = (8 + h // 2) * 128 + 64 * (h % 2)
        kb.rec("vector", lambda v, oq=oq: v.tensor_reduce(
            s11[0:1, S_T2:S_T2 + 1], r_row[0:1, oq:oq + 64], AX.X, ALU.max),
            reads=[gb("r_row")], writes=[gb("s11")])
        kb.rec("vector", lambda v, ok=ok: v.tensor_reduce(
            s11[0:1, S_T3:S_T3 + 1], r_row[0:1, ok:ok + 64], AX.X, ALU.max),
            reads=[gb("r_row")], writes=[gb("s11")])
        kb.rec("vector", lambda v: v.tensor_tensor(
            s11[0:1, S_T4:S_T4 + 1], s11[0:1, S_T2:S_T2 + 1],
            s11[0:1, S_T3:S_T3 + 1], ALU.mult),
            reads=[gb("s11")], writes=[gb("s11")])
        kb.rec("vector", lambda v: v.tensor_scalar(
            s11[0:1, S_T4:S_T4 + 1], s11[0:1, S_T4:S_T4 + 1], -8.08, None, ALU.mult),
            reads=[gb("s11")], writes=[gb("s11")])
        bcast(S_T4, P_BOUND0 + h)
    kb.end_phase()

    if tap == "qkv":
        _tapd(kb, gb, dbg, [(vsb, gb("vsb"), NTS * VW)])
        A.finalize()
        kb.finalize()
        nc._keep = (es,)
        return nc

    for h in range(NH):
        par = h % 2
        ec, half = h // 2, h % 2
        for nm, p_, hl in (("qh", 0, 0), ("ql", 0, 1), ("kh", 1, 0), ("kl", 1, 1)):
            off = (((p_ * DC + ec) * 2 + hl) * 128 + 64 * half) * TOK
            kb.dma("sync", f"qkld{par}", lambda s, nm=nm, off=off, par=par: s.dma_start(
                       out=qbuf[(nm, par)][:, :],
                       in_=bass.AP(qk_sp, off, [[TOK, 64], [1, TOK]])),
                reads=[gb(f"qksp{p_}_{ec}")], writes=[gb(f"qb_{nm}{par}")])
        for qt in range(NTT):
            zB = gb(f"PB{2 + qt % 2}")

            def rec_scores(kc, qt=qt, par=par, h=h):
                stB = gb(f"PB{kc % 2}")
                trip = ((qbuf[("kh", par)], gb(f"qb_kh{par}"), qbuf[("qh", par)], gb(f"qb_qh{par}")),
                        (qbuf[("kh", par)], gb(f"qb_kh{par}"), qbuf[("ql", par)], gb(f"qb_ql{par}")),
                        (qbuf[("kl", par)], gb(f"qb_kl{par}"), qbuf[("qh", par)], gb(f"qb_qh{par}")))
                for i, (kv, kB, qv, qB) in enumerate(trip):
                    kb.rec("tensor", lambda t, kv=kv, qv=qv, kc=kc,
                           qt=qt, i=i: t.matmul(
                               pb[kc % 2][:, :],
                               kv[:, kc * 128:(kc + 1) * 128],
                               qv[:, qt * 512:(qt + 1) * 512],
                               start=(i == 0), stop=(i == 2)),
                        reads=[kB, qB], writes=[stB])
                kb.rec("scalar", lambda s, kc=kc, h=h: s.activation(
                    out=pexp[kc % 4][:, :], in_=pb[kc % 2][:, :], func=AF.Exp,
                    scale=0.125, bias=pool[:, P_BOUND0 + h:P_BOUND0 + h + 1]),
                    reads=[stB], ptr_reads=[gb("pool")], writes=[gb(f"pe{kc % 4}")])

            def rec_av(kc, qt=qt, h=h, zB=zB):
                kb.rec("tensor", lambda t, kc=kc, qt=qt, h=h: t.matmul(
                    pb[2 + qt % 2][0:65, :],
                    vsb[:, kc * VW + 65 * h: kc * VW + 65 * h + 65],
                    pexp[kc % 4][:, :], start=(kc == 0), stop=(kc == NTS - 1)),
                    reads=[gb(f"pe{kc % 4}"), gb("vsb")], writes=[zB])

            for g in range(NTS // 4):
                for kcl in range(4):
                    rec_scores(g * 4 + kcl)
                for kcl in range(4):
                    rec_av(g * 4 + kcl)
            kb.rec("vector", lambda v, qt=qt: v.reciprocal(
                red1[0:1, 0:512], pb[2 + qt % 2][64:65, :]),
                reads=[zB], writes=[gb("red1")])
            kb.rec("tensor", lambda t, qt=qt: t.matmul(
                pb[4][0:64, :], ones_row[0:1, 0:64], red1[0:1, 0:512],
                start=True, stop=True),
                reads=[gb("red1"), gb("ones_row")], writes=[gb("PB4")])
            kb.rec("vector", lambda v, qt=qt: v.tensor_copy(
                zst[qt % 2][:, :], pb[2 + qt % 2][0:64, :]),
                reads=[zB], writes=[gb(f"zst{qt % 2}")])
            kb.rec("vector", lambda v, qt=qt: v.tensor_tensor(
                zot[qt % 2][:, :], zst[qt % 2][:, :],
                pb[4][0:64, :], ALU.mult),
                reads=[gb(f"zst{qt % 2}"), gb("PB4")], writes=[gb(f"zot{qt % 2}")])
            kb.dma("gpsimd", f"amv{qt % 2}", lambda g, h=h, qt=qt: g.dma_start(
                out=attn_sb[64 * (h % 2):64 * (h % 2) + 64,
                            (h // 2) * TOK + qt * 512:(h // 2) * TOK + qt * 512 + 512],
                in_=zot[qt % 2][:, :]),
                reads=[gb(f"zot{qt % 2}")], writes=[gb("attn")])
    kb.end_phase()

    if tap == "attn":
        _tapd(kb, gb, dbg, [(attn_sb, gb("attn"), DC * TOK)])
        A.finalize()
        kb.finalize()
        nc._keep = (es,)
        return nc

    slabs = []
    for wname in ("up", "gate", "down"):
        iw = 1024 if wname != "down" else II
        noc = II // 128 if wname != "down" else DC
        for oc in range(noc):
            slabs.append((wname, oc, iw, iw // 32, iw // 128))

    def p5_pre(k):
        wname, oc, iw, nb, njt = slabs[k]
        p = k % 2
        wb = gb(f"wnat{p}")
        kb.dma("sync", f"wfld{p}", lambda s, oc=oc, w_hbm=wffn_n[wname], iw=iw,
               p=p: s.dma_start(
            out=wnat[p][:, 0:iw], in_=w_hbm[oc * 128:(oc + 1) * 128, :]),
            writes=[wb])
        kb.rec("vector", lambda v, p=p, iw=iw, nb=nb: v.tensor_reduce(
            sdq[p][:, 0:nb], bass.AP(wnat[p], 0, [[II, 128], [32, nb], [1, 32]]),
            AX.X, ALU.max, apply_absolute_value=True),
            reads=[wb], writes=[gb(f"sdq{p}")])
        kb.rec("vector", lambda v, p=p, nb=nb: v.tensor_scalar(
            sdq[p][:, 0:nb], sdq[p][:, 0:nb], 1.0 / 7.0, 1e-12, ALU.mult, ALU.max),
            reads=[gb(f"sdq{p}")], writes=[gb(f"sdq{p}")])
        kb.rec("vector", lambda v, p=p, nb=nb: v.reciprocal(rdq[p][:, 0:nb], sdq[p][:, 0:nb]),
               reads=[gb(f"sdq{p}")], writes=[gb(f"rdq{p}")])

    def p5_mid(k):
        wname, oc, iw, nb, njt = slabs[k]
        p = k % 2
        wb = gb(f"wnat{p}")
        kb.rec("scalar", lambda s, p=p, iw=iw: s.activation(
            out=sgt[:, 0:iw], in_=wnat[p][:, 0:iw], func=AF.Sign),
            reads=[wb], writes=[gb("sgt")])
        kb.rec("scalar", lambda s, iw=iw: s.activation(
            out=sgt2[:, 0:iw], in_=sgt[:, 0:iw], func=AF.Identity,
            scale=pool[:, P_HALF:P_HALF + 1]),
            reads=[gb("sgt")], ptr_reads=[gb("pool")], writes=[gb("sgt2")])
        kb.rec("vector", lambda v, p=p, iw=iw, nb=nb: v.tensor_tensor(
            vt[:, 0:iw], wnat[p][:, 0:iw],
            bass.AP(rdq[p], 0, [[96, 128], [1, nb], [0, 32]]), ALU.mult),
            reads=[wb, gb(f"rdq{p}")], writes=[gb("vt")])

    def p5_core(k):
        wname, oc, iw, nb, njt = slabs[k]
        p = k % 2
        kb.rec("vector", lambda v, iw=iw: v.tensor_tensor(
            qt_[:, 0:iw], vt[:, 0:iw], sgt2[:, 0:iw], ALU.add),
            reads=[gb("vt"), gb("sgt2")], writes=[gb("qt_")])
        kb.rec("vector", lambda v, p=p, iw=iw, nb=nb: v.tensor_tensor(
            wdqt[:, 0:iw], qt_[:, 0:iw],
            bass.AP(sdq[p], 0, [[96, 128], [1, nb], [0, 32]]), ALU.mult),
            reads=[gb("qt_"), gb(f"sdq{p}")], writes=[gb("wdqt")])

    def p5_evac(k):
        wname, oc, iw, nb, njt = slabs[k]
        p = k % 2
        for j in range(njt):
            pbB = gb(f"PB{j % 4}")
            hB = gb(f"wtsR{p}_{j}")
            kb.rec("tensor", lambda t, j=j: t.transpose(
                pb[j % 4][:, 0:128], wdqt[:, j * 128:(j + 1) * 128],
                ident[:, :]),
                reads=[gb("wdqt"), gb("ident")], writes=[pbB])
            kb.rec("scalar", lambda s, j=j, p=p: s.activation(
                out=wts[p][:, j * 128:(j + 1) * 128], in_=pb[j % 4][:, 0:128],
                func=AF.Copy),
                reads=[pbB], writes=[hB])
            kb.rec("vector", lambda v, j=j, p=p, njt=njt: v.tensor_tensor(
                wts[p][:, (njt + j) * 128:(njt + j) * 128 + 128],
                pb[j % 4][:, 0:128], wts[p][:, j * 128:(j + 1) * 128],
                ALU.subtract),
                reads=[pbB, hB], writes=[gb(f"wtsR{p}_{njt + j}")])
        soff = oc * 2 * njt * 128 * 128
        kb.dma("gpsimd", f"wtsp{p}", lambda g, p=p, wname=wname, njt=njt,
               soff=soff: g.dma_start(
            out=bass.AP(wt_sp[wname], soff,
                        [[128, 128], [16384, 2 * njt], [1, 128]]),
            in_=wts[p][:, 0:2 * njt * 128]),
            reads=[gb(f"wtsR{p}_{i}") for i in range(2 * njt)],
            writes=[gb(f"wt_{wname}")])

    for k in range(len(slabs)):
        p5_pre(k)
        p5_mid(k)
        if k > 0:
            p5_evac(k - 1)
        p5_core(k)
    p5_evac(len(slabs) - 1)
    kb.end_phase()

    def p4_apply(tt, dc, rB, mrB):
        o = dc * TOK + tt * 512
        u = ust[2 + (dc % 2)]
        uB = gb(f"ust{2 + dc % 2}")
        xt = ust[dc % 2]
        xtB = gb(f"ust{dc % 2}")
        kb.dma("sync", f"xr{dc % 2}", lambda s: s.dma_start(
            out=xt[:, :], in_=x_fm[dc * 128:(dc + 1) * 128, tt * 512:(tt + 1) * 512]),
            writes=[xtB])
        kb.rec("vector", lambda v: v.tensor_tensor(
            u[:, :], attn_sb[:, o:o + 512], pb[3][:, :], ALU.mult),
            reads=[gb("attn"), rB], writes=[uB])
        kb.rec("vector", lambda v: v.tensor_tensor(
            u[:, :], u[:, :], pb[4][:, :], ALU.subtract),
            reads=[uB, mrB], writes=[uB])
        kb.rec("vector", lambda v: v.tensor_tensor(
            x2_sb[:, o:o + 512], u[:, :], xt[:, :], ALU.add),
            reads=[uB, xtB], writes=[gb("x2")])

    emit_ln("p4", attn_sb, gb("attn"), p4_apply)
    for dc in range(DC):
        kb.dma("gpsimd", "x2sp", lambda g, dc=dc: g.dma_start(
            out=x2_sp[dc], in_=x2_sb[:, dc * TOK:(dc + 1) * TOK]),
            reads=[gb("x2")], writes=[gb("x2_sp")])
    kb.end_phase()

    if tap == "x2":
        _tapd(kb, gb, dbg, [(x2_sb, gb("x2"), DC * TOK)])
        A.finalize()
        kb.finalize()
        nc._keep = (es,)
        return nc

    lnt = [ust[0], ust[1]]
    sgs = [ust[2], ust[3]]

    kb.rec("vector", lambda v: v.memset(gmax_acc[:, :], 0.0), writes=[gb("gmax")])

    def p6_pass1(tt, dc, rB, mrB):
        o = dc * TOK + tt * 512
        u = lnt[dc % 2]
        uB = gb(f"ust{dc % 2}")
        kb.rec("vector", lambda v: v.tensor_tensor(
            u[:, :], x2_sb[:, o:o + 512], pb[3][:, :], ALU.mult),
            reads=[gb("x2"), rB], writes=[uB])
        kb.rec("vector", lambda v: v.tensor_tensor(
            u[:, :], u[:, :], pb[4][:, :], ALU.subtract),
            reads=[uB, mrB], writes=[uB])
        kb.rec("vector", lambda v: v.tensor_reduce(
            gmax_acc[:, 2:3], u[:, :], AX.X, ALU.max, apply_absolute_value=True),
            reads=[uB], writes=[gb("gmax")])
        kb.rec("vector", lambda v: v.tensor_tensor(
            gmax_acc[:, 0:1], gmax_acc[:, 0:1], gmax_acc[:, 2:3], ALU.max),
            reads=[gb("gmax")], writes=[gb("gmax")])

    emit_ln("p6", x2_sb, gb("x2"), p6_pass1)
    cpmax(gmax_acc[:, 0:1], 1, [S_GM], [gb("gmax")])
    kb.rec("vector", lambda v: v.memset(ccst[:, :], 0.0), writes=[gb("ccst")])
    kb.rec("vector", lambda v: v.tensor_copy(ccst[0:1, 0:1], s11[0:1, S_GM:S_GM + 1]),
           reads=[gb("s11"), gb("ccst")], writes=[gb("ccst")])
    barrier(1)
    kb.rec("vector", lambda v: v.tensor_scalar(
        s11[0:1, S_SA:S_SA + 1], ccst[0:1, 0:1], 1.0 / QMAX, 1e-12, ALU.mult, ALU.max),
        reads=[gb("ccst"), gb("s11")], writes=[gb("s11")])
    kb.rec("vector", lambda v: v.reciprocal(
        s11[0:1, S_TMP:S_TMP + 1], s11[0:1, S_SA:S_SA + 1]),
        reads=[gb("s11")], writes=[gb("s11")])
    bcast(S_TMP, P_INVSA)

    rB2, mrB2 = gb("PB3"), gb("PB4")
    for tt in range(NTT):
        kb.rec("tensor", lambda t, tt=tt: t.matmul(
            pb[3][:, :], ones_row[0:1, :], r_row[0:1, tt * 512:(tt + 1) * 512],
            start=True, stop=True),
            reads=[gb("r_row"), gb("ones_row")], writes=[rB2])
        kb.rec("tensor", lambda t, tt=tt: t.matmul(
            pb[4][:, :], ones_row[0:1, :], mr_row[0:1, tt * 512:(tt + 1) * 512],
            start=True, stop=True),
            reads=[gb("mr_row"), gb("ones_row")], writes=[mrB2])
        for dc in range(DC):
            o = dc * TOK + tt * 512
            u, sg, nx = lnt[dc % 2], sgs[dc % 2], nxi[dc % 2]
            uB, sgB, nxB = gb(f"ust{dc % 2}"), gb(f"ust{2 + dc % 2}"), gb(f"nxi{dc % 2}")
            kb.rec("vector", lambda v, o=o, u=u: v.tensor_tensor(
                u[:, :], x2_sb[:, o:o + 512], pb[3][:, :], ALU.mult),
                reads=[gb("x2"), rB2], writes=[uB])
            kb.rec("vector", lambda v, u=u: v.tensor_tensor(
                u[:, :], u[:, :], pb[4][:, :], ALU.subtract),
                reads=[uB, mrB2], writes=[uB])
            kb.rec("scalar", lambda s, u=u, sg=sg: s.activation(
                out=sg[:, :], in_=u[:, :], func=AF.Sign),
                reads=[uB], writes=[sgB])
            kb.rec("scalar", lambda s, sg=sg: s.activation(
                out=sg[:, :], in_=sg[:, :], func=AF.Identity,
                scale=pool[:, P_HALF:P_HALF + 1]),
                reads=[sgB], ptr_reads=[gb("pool")], writes=[sgB])
            kb.rec("vector", lambda v, u=u, sg=sg, nx=nx: v.scalar_tensor_tensor(
                nx[:, :], u[:, :], pool[:, P_INVSA:P_INVSA + 1], sg[:, :],
                ALU.mult, ALU.add),
                reads=[uB, sgB], ptr_reads=[gb("pool")], writes=[nxB])
            kb.rec("scalar", lambda s, o=o, nx=nx: s.activation(
                out=nxh_sb[:, o:o + 512], in_=nx[:, :], func=AF.Copy),
                reads=[nxB], writes=[gb("nxh")])
            kb.rec("vector", lambda v, o=o, nx=nx: v.tensor_tensor(
                nxl_sb[:, o:o + 512], nx[:, :], nxh_sb[:, o:o + 512], ALU.subtract),
                reads=[nxB, gb("nxh")], writes=[gb("nxl")])
    kb.end_phase()

    if tap == "ln2":
        for dc in range(DC):
            kb.rec("vector", lambda v, dc=dc: v.tensor_tensor(
                x2_sb[:, dc * TOK:(dc + 1) * TOK], nxh_sb[:, dc * TOK:(dc + 1) * TOK],
                nxl_sb[:, dc * TOK:(dc + 1) * TOK], ALU.add),
                reads=[gb("nxh"), gb("nxl")], writes=[gb("x2")])
        _tapd(kb, gb, dbg, [(x2_sb, gb("x2"), DC * TOK)])
        A.finalize()
        kb.finalize()
        nc._keep = (es,)
        return nc

    def emit_round(srct, srcB, sg, sgB, nx, nxB, inv_col, width=TOK, eng="vector"):
        halfc = P_HALF if eng == "vector" else P_HALFG
        kb.rec("scalar", lambda s: s.activation(
            out=sg[:, 0:width], in_=srct[:, 0:width], func=AF.Sign),
            reads=[srcB], writes=[sgB])
        kb.rec("scalar", lambda s, halfc=halfc: s.activation(
            out=sg[:, 0:width], in_=sg[:, 0:width], func=AF.Identity,
            scale=pool[:, halfc:halfc + 1]),
            reads=[sgB], ptr_reads=[gb("pool")], writes=[sgB])
        kb.rec(eng, lambda v: v.scalar_tensor_tensor(
            nx[:, 0:width], srct[:, 0:width], pool[:, inv_col:inv_col + 1],
            sg[:, 0:width], ALU.mult, ALU.add),
            reads=[srcB, sgB], ptr_reads=[gb("pool")], writes=[nxB])

    def cpmax2(col, slot, bufs):
        kb.rec("tensor", lambda tt_, col=col: tt_.transpose(
            pb[2][0:1, 0:128], gmax_acc[:, col:col + 1], ident[:, :]),
            reads=list(bufs) + [gb("ident")], writes=[gb("PB2")])
        kb.rec("vector", lambda v, slot=slot: v.tensor_reduce(
            s11[0:1, slot:slot + 1], pb[2][0:1, 0:128],
            AX.X, ALU.max, apply_absolute_value=True),
            reads=[gb("PB2")], writes=[gb("s11")])

    kb.rec("vector", lambda v: v.memset(gmax_acc[:, :], 0.0),
           reads=[gb("gmax")], writes=[gb("gmax")])

    def s1_iter(wname, wi, ic):
        dst_sp = act_sp[wname]
        wB = gb(f"wstr{ic % 2}")
        kb.dma("sync", f"ws1_{ic % 2}", lambda s, ic=ic, wname=wname: s.dma_start(
            out=wstr[ic % 2][:, :],
            in_=bass.AP(wt_sp[wname], ic * 2 * DC * 128 * 128,
                        [[128, 128], [16384, 2 * DC], [1, 128]])),
            reads=[gb(f"wt_{wname}")], writes=[wB])
        sg_ = fstg[ic % 2]
        sgB_ = gb(f"fstg{ic % 2}")
        for tt in range(NTT):
            pk = gb(f"PB{tt % 2}")
            mms = []
            for hl, xbuf, xB in ((0, nxh_sb, gb("nxh")), (0, nxl_sb, gb("nxl")),
                                 (1, nxh_sb, gb("nxh"))):
                for dc in range(DC):
                    mms.append((hl, xbuf, xB, dc))
            for i, (hl, xbuf, xB, dc) in enumerate(mms):
                kb.rec("tensor", lambda t, hl=hl, xbuf=xbuf, dc=dc, i=i,
                       ic=ic, tt=tt, n=len(mms): t.matmul(
                           pb[tt % 2][:, :],
                           wstr[ic % 2][:, (hl * DC + dc) * 128:(hl * DC + dc) * 128 + 128],
                           xbuf[:, dc * TOK + tt * 512: dc * TOK + tt * 512 + 512],
                           start=(i == 0), stop=(i == n - 1)),
                    reads=[wB, xB], writes=[pk])
            kb.rec("scalar", lambda s, tt=tt, sg_=sg_: s.activation(
                out=sg_[:, tt * 512:(tt + 1) * 512], in_=pb[tt % 2][:, :],
                func=AF.Copy), reads=[pk], writes=[sgB_])
        kb.rec("vector", lambda v, sg_=sg_: v.tensor_reduce(
            gmax_acc[:, 3:4], sg_[:, :], AX.X, ALU.max, apply_absolute_value=True),
            reads=[sgB_], writes=[gb("gmax")])
        kb.rec("vector", lambda v, wi=wi: v.tensor_tensor(
            gmax_acc[:, wi:wi + 1], gmax_acc[:, wi:wi + 1], gmax_acc[:, 3:4],
            ALU.max), reads=[gb("gmax")], writes=[gb("gmax")])
        kb.dma("gpsimd", f"s1sp{ic % 2}", lambda g, ic=ic, dst_sp=dst_sp,
               sg_=sg_: g.dma_start(
            out=dst_sp[ic], in_=sg_[:, :]),
            reads=[sgB_], writes=[gb(f"sp_{wname}")])

    def s2_iter(ic):
        gB = gb(f"gsb{ic % 2}")
        kb.dma("sync", f"s2ld{ic % 2}", lambda s, ic=ic: s.dma_start(
            out=gsb[ic % 2][:, :], in_=act_sp["gate"][ic]),
            reads=[gb("sp_gate")], writes=[gB])
        emit_round(gsb[ic % 2], gB, sgf, gb("sgf"), ngi, gb("ngi"), P_INVG)
        kb.rec("scalar", lambda s: s.activation(
            out=sigf[:, :], in_=ngi[:, :], func=AF.Sigmoid,
            scale=pool[:, P_SG:P_SG + 1]),
            reads=[gb("ngi")], ptr_reads=[gb("pool")], writes=[gb("sigf")])
        kb.rec("vector", lambda v: v.tensor_scalar(
            sigf[:, :], sigf[:, :], 65536.0, pool[:, P_HALF:P_HALF + 1],
            ALU.mult, ALU.add),
            reads=[gb("sigf")], ptr_reads=[gb("pool")], writes=[gb("sigf")])
        kb.rec("vector", lambda v: v.tensor_scalar(
            mi[:, :], sigf[:, :], 65535.0, None, ALU.min),
            reads=[gb("sigf")], writes=[gb("mi")])
        kb.rec("vector", lambda v, ic=ic: v.tensor_tensor(
            g2f[ic % 2][:, :], ngi[:, :], mi[:, :], ALU.mult),
            reads=[gb("ngi"), gb("mi")], writes=[gb(f"g2f{ic % 2}")])
        kb.rec("vector", lambda v, ic=ic: v.tensor_reduce(
            gmax_acc[:, 3:4], g2f[ic % 2][:, :], AX.X, ALU.max,
            apply_absolute_value=True),
            reads=[gb(f"g2f{ic % 2}")], writes=[gb("gmax")])
        kb.rec("vector", lambda v: v.tensor_tensor(
            gmax_acc[:, 2:3], gmax_acc[:, 2:3], gmax_acc[:, 3:4], ALU.max),
            reads=[gb("gmax")], writes=[gb("gmax")])
        kb.dma("gpsimd", f"s2sp{ic % 2}", lambda g, ic=ic: g.dma_start(
            out=act_sp["gate2"][ic], in_=g2f[ic % 2][:, :]),
            reads=[gb(f"g2f{ic % 2}")], writes=[gb("sp_gate2")])

    for ic in range(IC):
        s1_iter("gate", 0, ic)
    cpmax2(0, S_GM, [gb("gmax")])
    kb.rec("vector", lambda v: v.memset(ccst[:, :], 0.0),
           reads=[gb("ccst")], writes=[gb("ccst")])
    kb.rec("vector", lambda v: v.tensor_scalar(
        ccst[0:1, 0:1], s11[0:1, S_GM:S_GM + 1], s11[0:1, S_SA:S_SA + 1], None,
        ALU.mult), reads=[gb("s11")], ptr_reads=[gb("s11")], writes=[gb("ccst")])
    barrier(1)
    kb.rec("vector", lambda v: v.tensor_scalar(
        s11[0:1, S_SGV:S_SGV + 1], ccst[0:1, 0:1], 1.0 / QMAX, 1e-12, ALU.mult,
        ALU.max), reads=[gb("ccst")], writes=[gb("s11")])
    kb.rec("vector", lambda v: v.reciprocal(
        s11[0:1, S_TMP:S_TMP + 1], s11[0:1, S_SGV:S_SGV + 1]),
        reads=[gb("s11")], writes=[gb("s11")])
    kb.rec("vector", lambda v: v.tensor_scalar(
        s11[0:1, S_T2:S_T2 + 1], s11[0:1, S_TMP:S_TMP + 1],
        s11[0:1, S_SA:S_SA + 1], None, ALU.mult),
        reads=[gb("s11")], ptr_reads=[gb("s11")], writes=[gb("s11")])
    kb.rec("vector", lambda v: v.tensor_scalar(
        s11[0:1, S_C2:S_C2 + 1], s11[0:1, S_SGV:S_SGV + 1], 1.0 / 65536.0, None,
        ALU.mult), reads=[gb("s11")], writes=[gb("s11")])
    bcast(S_SGV, P_SG)
    bcast(S_T2, P_INVG)
    for ic in range(IC):
        s1_iter("up", 1, ic)
        if ic >= 1:
            s2_iter(ic - 1)
    cpmax2(1, S_GM2, [gb("gmax")])
    kb.rec("vector", lambda v: v.memset(ccst[:, :], 0.0),
           reads=[gb("ccst")], writes=[gb("ccst")])
    kb.rec("vector", lambda v: v.tensor_scalar(
        ccst[0:1, 0:1], s11[0:1, S_GM2:S_GM2 + 1], s11[0:1, S_SA:S_SA + 1], None,
        ALU.mult), reads=[gb("s11")], ptr_reads=[gb("s11")], writes=[gb("ccst")])
    barrier(1)
    kb.rec("vector", lambda v: v.tensor_scalar(
        s11[0:1, S_SUV:S_SUV + 1], ccst[0:1, 0:1], 1.0 / QMAX, 1e-12, ALU.mult,
        ALU.max), reads=[gb("ccst")], writes=[gb("s11")])
    kb.rec("vector", lambda v: v.reciprocal(
        s11[0:1, S_TMP:S_TMP + 1], s11[0:1, S_SUV:S_SUV + 1]),
        reads=[gb("s11")], writes=[gb("s11")])
    kb.rec("vector", lambda v: v.tensor_scalar(
        s11[0:1, S_T3:S_T3 + 1], s11[0:1, S_TMP:S_TMP + 1],
        s11[0:1, S_SA:S_SA + 1], None, ALU.mult),
        reads=[gb("s11")], ptr_reads=[gb("s11")], writes=[gb("s11")])
    bcast(S_T3, P_INVU)
    s2_iter(IC - 1)
    cpmax2(2, S_GM, [gb("gmax")])
    kb.rec("vector", lambda v: v.memset(ccst[:, :], 0.0),
           reads=[gb("ccst")], writes=[gb("ccst")])
    kb.rec("vector", lambda v: v.tensor_scalar(
        ccst[0:1, 0:1], s11[0:1, S_GM:S_GM + 1], s11[0:1, S_C2:S_C2 + 1], None,
        ALU.mult), reads=[gb("s11")], ptr_reads=[gb("s11")], writes=[gb("ccst")])
    barrier(1)
    kb.rec("vector", lambda v: v.tensor_scalar(
        s11[0:1, S_T4:S_T4 + 1], ccst[0:1, 0:1], 1.0 / QMAX, 1e-12, ALU.mult,
        ALU.max), reads=[gb("ccst")], writes=[gb("s11")])
    kb.rec("vector", lambda v: v.reciprocal(
        s11[0:1, S_TMP:S_TMP + 1], s11[0:1, S_T4:S_T4 + 1]),
        reads=[gb("s11")], writes=[gb("s11")])
    kb.rec("vector", lambda v: v.tensor_scalar(
        s11[0:1, S_T5:S_T5 + 1], s11[0:1, S_TMP:S_TMP + 1],
        s11[0:1, S_C2:S_C2 + 1], None, ALU.mult),
        reads=[gb("s11")], ptr_reads=[gb("s11")], writes=[gb("s11")])
    kb.rec("vector", lambda v: v.tensor_scalar(
        s11[0:1, S_C3:S_C3 + 1], s11[0:1, S_T4:S_T4 + 1],
        s11[0:1, S_SUV:S_SUV + 1], None, ALU.mult),
        reads=[gb("s11")], ptr_reads=[gb("s11")], writes=[gb("s11")])
    bcast(S_T5, P_INVG2)
    kb.end_phase()

    if tap == "s2":
        A.finalize()
        kb.finalize()
        nc._keep = (es,)
        return nc

    kb.rec("vector", lambda v: v.memset(gmax_acc[:, :], 0.0),
           reads=[gb("gmax")], writes=[gb("gmax")])
    for ic in range(IC):
        gB = gb(f"gsb{ic % 2}")
        uB = gb(f"usb{ic % 2}")
        kb.dma("sync", f"s3ld{ic % 2}", lambda s, ic=ic: s.dma_start(
            out=gsb[ic % 2][:, :], in_=act_sp["gate2"][ic]),
            reads=[gb("sp_gate2")], writes=[gB])
        kb.dma("sync", f"s3ld{ic % 2}", lambda s, ic=ic: s.dma_start(
            out=usb[ic % 2][:, :], in_=act_sp["up"][ic]),
            reads=[gb("sp_up")], writes=[uB])
        emit_round(gsb[ic % 2], gB, sgf, gb("sgf"), ngi, gb("ngi"), P_INVG2)
        emit_round(usb[ic % 2], uB, sigf, gb("sigf"), nui, gb("nui"), P_INVU)
        kb.rec("vector", lambda v, ic=ic: v.tensor_tensor(
            of[ic % 2][:, :], ngi[:, :], nui[:, :], ALU.mult),
            reads=[gb("ngi"), gb("nui")], writes=[gb(f"of{ic % 2}")])
        kb.rec("vector", lambda v, ic=ic: v.tensor_reduce(
            gmax_acc[:, 2:3], of[ic % 2][:, :], AX.X, ALU.max,
            apply_absolute_value=True),
            reads=[gb(f"of{ic % 2}")], writes=[gb("gmax")])
        kb.rec("vector", lambda v: v.tensor_tensor(
            gmax_acc[:, 0:1], gmax_acc[:, 0:1], gmax_acc[:, 2:3], ALU.max),
            reads=[gb("gmax")], writes=[gb("gmax")])
        kb.dma("gpsimd", f"s3sp{ic % 2}", lambda g, ic=ic: g.dma_start(
            out=act_sp["o"][ic], in_=of[ic % 2][:, :]),
            reads=[gb(f"of{ic % 2}")], writes=[gb("sp_o")])
    cpmax(gmax_acc[:, 0:1], 1, [S_GM], [gb("gmax")])
    kb.rec("vector", lambda v: v.memset(ccst[:, :], 0.0),
           reads=[gb("ccst")], writes=[gb("ccst")])
    kb.rec("vector", lambda v: v.tensor_scalar(
        ccst[0:1, 0:1], s11[0:1, S_GM:S_GM + 1], s11[0:1, S_C3:S_C3 + 1], None,
        ALU.mult), reads=[gb("s11")], ptr_reads=[gb("s11")], writes=[gb("ccst")])
    barrier(1)
    kb.rec("vector", lambda v: v.tensor_scalar(
        s11[0:1, S_T4:S_T4 + 1], ccst[0:1, 0:1], 1.0 / QMAX, 1e-12, ALU.mult,
        ALU.max), reads=[gb("ccst")], writes=[gb("s11")])
    kb.rec("vector", lambda v: v.reciprocal(
        s11[0:1, S_TMP:S_TMP + 1], s11[0:1, S_T4:S_T4 + 1]),
        reads=[gb("s11")], writes=[gb("s11")])
    kb.rec("vector", lambda v: v.tensor_scalar(
        s11[0:1, S_T5:S_T5 + 1], s11[0:1, S_TMP:S_TMP + 1],
        s11[0:1, S_C3:S_C3 + 1], None, ALU.mult),
        reads=[gb("s11")], ptr_reads=[gb("s11")], writes=[gb("s11")])
    bcast(S_T5, P_INVO)
    kb.end_phase()

    if tap == "s3":
        A.finalize()
        kb.finalize()
        nc._keep = (es,)
        return nc

    x2t = [ust[2], ust[3]]
    otg = [ust[0], ust[1]]

    def wd_slice(hc, hl, ic):
        if hc < 4:
            return wdsbA[:, ((hc * 2 + hl) * IC + ic) * 128:
                         ((hc * 2 + hl) * IC + ic) * 128 + 128]
        return wdsbB[:, (((hc - 4) * 2 + hl) * IC + ic) * 128:
                     (((hc - 4) * 2 + hl) * IC + ic) * 128 + 128]

    NS4 = 4

    for hc in range(DC):
        dst = wd_slice(hc, 0, 0).tensor if False else (wdsbA if hc < 4 else wdsbB)
        base = (hc if hc < 4 else hc - 4) * 2 * IC * 128
        kb.dma("sync", "wdld", lambda s, hc=hc, dst=dst, base=base: s.dma_start(
            out=dst[:, base:base + 2 * IC * 128],
            in_=bass.AP(wt_sp["down"], hc * 2 * IC * 128 * 128,
                        [[128, 128], [16384, 2 * IC], [1, 128]])),
            reads=[gb("wt_down")], writes=[gb("wdsb")])




    def p10_prep(tt):
        for ic in range(IC):
            oB = gb(f"ost{ic % 2}")
            kb.dma("sync", f"s4ld{ic % 2}", lambda s, ic=ic, tt=tt: s.dma_start(
                out=ost[ic % 2][:, :],
                in_=bass.AP(act_sp["o"], ic * 128 * TOK + tt * 512,
                            [[TOK, 128], [1, 512]])),
                reads=[gb("sp_o")], writes=[oB])
            emit_round(ost[ic % 2], oB, sgo[ic % 2], gb(f"sgo{ic % 2}"),
                       noi[ic % 2], gb(f"noi{ic % 2}"), P_INVO, width=512)
            kb.rec("scalar", lambda s, ic=ic: s.activation(
                out=noh[:, ic * 512:(ic + 1) * 512], in_=noi[ic % 2][:, :],
                func=AF.Copy), reads=[gb(f"noi{ic % 2}")], writes=[gb(f"nohT{ic}")])
            kb.rec("vector", lambda v, ic=ic: v.tensor_tensor(
                nol[:, ic * 512:(ic + 1) * 512], noi[ic % 2][:, :],
                noh[:, ic * 512:(ic + 1) * 512], ALU.subtract),
                reads=[gb(f"noi{ic % 2}"), gb(f"nohT{ic}")], writes=[gb("nol")])

    p10_prep(0)
    for tt in range(NTT):
        for hc in range(DC):
            pk = gb(f"PB{hc % 2}")
            mms = []
            for hl, xbuf in ((0, noh), (0, nol), (1, noh)):
                for ic in range(IC):
                    xB = gb(f"nohT{ic}") if xbuf is noh else gb("nol")
                    mms.append((hl, xbuf, xB, ic))
            for i, (hl, xbuf, xB, ic) in enumerate(mms):
                kb.rec("tensor", lambda t, hl=hl, xbuf=xbuf, ic=ic, i=i, hc=hc,
                       n=len(mms): t.matmul(
                           pb[hc % 2][:, :],
                           wd_slice(hc, hl, ic),
                           xbuf[:, ic * 512:(ic + 1) * 512],
                           start=(i == 0), stop=(i == n - 1)),
                    reads=[gb("wdsb"), xB], writes=[pk])
            kb.rec("scalar", lambda s, hc=hc: s.activation(
                out=ysb[:, hc * 512:(hc + 1) * 512], in_=pb[hc % 2][:, :],
                func=AF.Copy), reads=[pk], writes=[gb("ysb")])
        if tt + 1 < NTT:
            p10_prep(tt + 1)
        sumB, sqpB = gb("PB2"), gb("PB3")
        for hc in range(DC):
            sqb = gb(f"sq5{hc % 2}")
            kb.rec("scalar", lambda s, hc=hc: s.activation(
                out=sq5[hc % 2][:, :], in_=ysb[:, hc * 512:(hc + 1) * 512],
                func=AF.Square), reads=[gb("ysb")], writes=[sqb])
            kb.rec("tensor", lambda t, hc=hc: t.matmul(
                pb[2][0:1, :], ones_col[:, :], ysb[:, hc * 512:(hc + 1) * 512],
                start=(hc == 0), stop=(hc == DC - 1)),
                reads=[gb("ysb"), gb("ones_col")], writes=[sumB])
            kb.rec("tensor", lambda t, hc=hc: t.matmul(
                pb[3][0:1, :], ones_col[:, :], sq5[hc % 2][:, :],
                start=(hc == 0), stop=(hc == DC - 1)),
                reads=[sqb, gb("ones_col")], writes=[sqpB])
        rAB, rBB = gb("rr4"), gb("mr4")
        r2B = gb("red2")
        ep1 = pool[0:1, P_EPSLN:P_EPSLN + 1]
        kb.rec("vector", lambda v: v.tensor_scalar(
            rr4[0:1, :], pb[2][0:1, :], 1.0 / D, None, ALU.mult),
            reads=[sumB], writes=[rAB])
        kb.rec("vector", lambda v: v.tensor_scalar(
            mr4[0:1, :], pb[3][0:1, :], 1.0 / D, None, ALU.mult),
            reads=[sqpB], writes=[rBB])
        kb.rec("scalar", lambda s: s.activation(
            out=red2[0:1, :], in_=rr4[0:1, :], func=AF.Square),
            reads=[rAB], writes=[r2B])
        kb.rec("vector", lambda v: v.tensor_tensor(
            mr4[0:1, :], mr4[0:1, :], red2[0:1, :], ALU.subtract),
            reads=[rBB, r2B], writes=[rBB])
        kb.rec("scalar", lambda s: s.activation(
            out=red2[0:1, :], in_=mr4[0:1, :], func=AF.Sqrt, bias=ep1),
            reads=[rBB], ptr_reads=[gb("pool")], writes=[r2B])
        kb.rec("vector", lambda v: v.reciprocal(red1[0:1, :], red2[0:1, :]),
               reads=[r2B], writes=[gb("red1")])
        kb.rec("vector", lambda v: v.tensor_scalar(
            red2[0:1, :], mr4[0:1, :], 1.0, ep1, ALU.mult, ALU.add),
            reads=[rBB], ptr_reads=[gb("pool")], writes=[r2B])
        kb.rec("vector", lambda v: v.tensor_tensor(
            mr4[0:1, :], red1[0:1, :], red1[0:1, :], ALU.mult),
            reads=[gb("red1")], writes=[rBB])
        kb.rec("vector", lambda v: v.tensor_tensor(
            mr4[0:1, :], mr4[0:1, :], red2[0:1, :], ALU.mult),
            reads=[rBB, r2B], writes=[rBB])
        kb.rec("vector", lambda v: v.tensor_scalar(
            mr4[0:1, :], mr4[0:1, :], -0.5, 1.5, ALU.mult, ALU.add),
            reads=[rBB], writes=[rBB])
        kb.rec("vector", lambda v: v.tensor_tensor(
            red1[0:1, :], red1[0:1, :], mr4[0:1, :], ALU.mult),
            reads=[gb("red1"), rBB], writes=[gb("red1")])
        kb.rec("vector", lambda v: v.tensor_tensor(
            mr4[0:1, :], rr4[0:1, :], red1[0:1, :], ALU.mult),
            reads=[rAB, gb("red1")], writes=[rBB])
        kb.rec("tensor", lambda t: t.matmul(
            pb[2][:, :], ones_row[0:1, :], red1[0:1, :], start=True, stop=True),
            reads=[gb("red1"), gb("ones_row")], writes=[sumB])
        kb.rec("tensor", lambda t: t.matmul(
            pb[3][:, :], ones_row[0:1, :], mr4[0:1, :], start=True, stop=True),
            reads=[rBB, gb("ones_row")], writes=[sqpB])
        for hc in range(DC):
            xtB = gb(f"ust{2 + hc % 2}")
            kb.dma("sync", f"x2ld{hc % 2}", lambda s, hc=hc, tt=tt: s.dma_start(
                out=x2t[hc % 2][:, :],
                in_=bass.AP(x2_sp, hc * 128 * TOK + tt * 512,
                            [[TOK, 128], [1, 512]])),
                reads=[gb("x2_sp")], writes=[xtB])
            oB = gb(f"ust{hc % 2}")
            kb.rec("vector", lambda v, hc=hc: v.tensor_tensor(
                otg[hc % 2][:, :], ysb[:, hc * 512:(hc + 1) * 512], pb[2][:, :],
                ALU.mult), reads=[gb("ysb"), sumB], writes=[oB])
            kb.rec("vector", lambda v, hc=hc: v.tensor_tensor(
                otg[hc % 2][:, :], otg[hc % 2][:, :], pb[3][:, :], ALU.subtract),
                reads=[oB, sqpB], writes=[oB])
            kb.rec("vector", lambda v, hc=hc: v.tensor_tensor(
                otg[hc % 2][:, :], otg[hc % 2][:, :], x2t[hc % 2][:, :], ALU.add),
                reads=[oB, xtB], writes=[oB])
            kb.dma("gpsimd", f"ost{hc % 2}", lambda g, hc=hc, tt=tt: g.dma_start(
                out=out_fm[hc * 128:(hc + 1) * 128, tt * 512:(tt + 1) * 512],
                in_=otg[hc % 2][:, :]),
                reads=[oB], writes=[gb("OUT")])
    kb.rec("vector", lambda v: v.engine_nop(), reads=[gb("OUT")])
    kb.end_phase()
    A.finalize()
    kb.finalize()
    nc._keep = (es,)
    return nc


def _tapd(kb, gb, dbg, items):
    for (src, srcB, ncols) in items:
        for dc in range(0, ncols, TOK):
            w = min(TOK, ncols - dc)
            kb.dma("gpsimd", "dbg", lambda g, src=src, dc=dc, w=w: g.dma_start(
                out=dbg[:, dc:dc + w], in_=src[:, dc:dc + w]),
                reads=[srcB], writes=[gb("DBG")])
    kb.rec("vector", lambda v: v.engine_nop(), reads=[gb("DBG")])


def _prep_inputs(L0_attn_x, L0_attn_kernel, L0_ffn_w_gate, L0_ffn_w_up,
                 L0_ffn_w_down):
    x = np.asarray(L0_attn_x, dtype=np.float32)
    k3 = np.asarray(L0_attn_kernel, dtype=np.float32)
    wg = np.ascontiguousarray(np.asarray(L0_ffn_w_gate, dtype=np.float32))
    wu = np.ascontiguousarray(np.asarray(L0_ffn_w_up, dtype=np.float32))
    wd = np.ascontiguousarray(np.asarray(L0_ffn_w_down, dtype=np.float32))
    in_maps = []
    for c in range(8):
        in_maps.append({
            "x_fm": np.ascontiguousarray(x[c].T),
            "wq": np.ascontiguousarray(k3[0]),
            "wk": np.ascontiguousarray(k3[1]),
            "wv": np.ascontiguousarray(k3[2]),
            "wup_n": wu,
            "wgate_n": wg,
            "wdown_n": wd,
        })
    return in_maps


def kernel(L0_attn_x, L0_attn_kernel, L0_ffn_w_gate, L0_ffn_w_up, L0_ffn_w_down):
    in_maps = _prep_inputs(L0_attn_x, L0_attn_kernel, L0_ffn_w_gate,
                           L0_ffn_w_up, L0_ffn_w_down)
    nc = build()
    res = run_bass_kernel_spmd(nc, in_maps, core_ids=list(range(8)))
    out = np.stack([np.asarray(r["out_fm"]).T for r in res.results], axis=0)
    return np.ascontiguousarray(out.astype(np.float32))
